# version 03 (final)
# baseline (speedup 1.0000x reference)
import numpy as np
import ml_dtypes
import concourse.bass as bass
import concourse.mybir as mybir
from concourse.bass_utils import run_bass_kernel_spmd

F32 = mybir.dt.float32
BF16 = mybir.dt.bfloat16
AF = mybir.ActivationFunctionType
ALU = mybir.AluOpType
AX = mybir.AxisListType
BF = ml_dtypes.bfloat16

DM = 2048
NCH = 16
TPC = 2048
NH = 32
HD = 64
NPAIR = 16
BS = 128
NBLK = TPC // BS
DFF = 8192
SLC = DFF // 8
CSL = DM // 8
EPS_OUT = 1e-5
EPS_FFN = 1e-6
EPS_RMS = 1e-5
SEC = 256
NSEC = TPC // SEC
KSTEPS = 4

ENGS = ("tensor", "vector", "scalar", "sync", "gpsimd")



def _img(w):
    K, N = w.shape
    return np.ascontiguousarray(
        w.reshape(K // 128, 128, N).transpose(1, 0, 2).reshape(128, -1))


def _pp(v):
    return np.ascontiguousarray(np.asarray(v, np.float32).reshape(NCH, 128).T)


def _slopes():
    base = 1.0 / 2.0 ** (8.0 / NH)
    rate = base ** (np.arange(NH, dtype=np.float64) + 1.0)
    return rate * (1.0 + 1e-05)


def host_prep(inputs):
    x = np.asarray(inputs["L0_attn_hidden_states"], np.float32)
    qkv_w = np.asarray(inputs["L0_attn_qkv_w"], np.float32)
    out_w = np.asarray(inputs["L0_attn_out_w"], np.float32)
    gw_a = np.asarray(inputs["L0_attn_gate_w"], np.float32)
    norm_w = np.asarray(inputs["L0_attn_norm_w"], np.float32)
    in_g = np.asarray(inputs["L0_ffn_in_gamma"], np.float32)
    in_b = np.asarray(inputs["L0_ffn_in_beta"], np.float32)
    agg_w = np.asarray(inputs["L0_ffn_agg_w"], np.float32)
    agg_b = np.asarray(inputs["L0_ffn_agg_b"], np.float32)
    pn_g = np.asarray(inputs["L0_ffn_pn_gamma"], np.float32)
    pn_b = np.asarray(inputs["L0_ffn_pn_beta"], np.float32)
    d1_w = np.asarray(inputs["L0_ffn_d1_w"], np.float32)
    d1_b = np.asarray(inputs["L0_ffn_d1_b"], np.float32)
    d2_w = np.asarray(inputs["L0_ffn_d2_w"], np.float32)
    d2_b = np.asarray(inputs["L0_ffn_d2_b"], np.float32)
    bc_w = np.asarray(inputs["L0_ffn_bc_w"], np.float32)
    bc_b = np.asarray(inputs["L0_ffn_bc_b"], np.float32)
    gw_f = np.asarray(inputs["L0_ffn_gate_w"], np.float32)
    gate_b = np.asarray(inputs["L0_ffn_gate_b"], np.float32)
    out_g = np.asarray(inputs["L0_ffn_out_gamma"], np.float32)
    out_b = np.asarray(inputs["L0_ffn_out_beta"], np.float32)

    sh = {}
    rows = np.arange(NH)[:, None] * 192 + np.arange(HD)[None, :]
    for name, off in (("wq", 0), ("wk", 64), ("wv", 128)):
        rr = (rows + off).reshape(-1)
        wt = qkv_w[rr, :].T
        sh[name] = np.ascontiguousarray(np.stack(
            [_img(np.ascontiguousarray(wt[:, j * 128:(j + 1) * 128]))
             for j in range(NPAIR)]))
    ga_t = gw_a.T
    sh["wga"] = np.ascontiguousarray(np.stack(
        [_img(np.ascontiguousarray(ga_t[:, j * 128:(j + 1) * 128]))
         for j in range(NCH)]))
    wo_t = out_w.T
    sh["wo"] = np.ascontiguousarray(np.stack(
        [_img(np.ascontiguousarray(wo_t[:, j * 128:(j + 1) * 128]))
         for j in range(NCH)]))
    sh["wgf"] = _img(gw_f)
    sh["normw_pp"] = _pp(norm_w)
    sh["ing_pp"] = _pp(in_g)
    sh["inb_pp"] = _pp(in_b)
    sh["png_pp"] = _pp(pn_g)
    sh["pnb_pp"] = _pp(pn_b)
    sh["outg_pp"] = _pp(out_g)
    sh["outb_pp"] = _pp(out_b)
    sh["gfb_pp"] = _pp(gate_b)
    sh["d2b_row"] = np.ascontiguousarray(d2_b[None, :])
    sh["xlast4"] = np.ascontiguousarray(x[:, -1, :])

    s = _slopes()
    i = np.arange(BS, dtype=np.float64)
    ddt = np.zeros((NPAIR, BS, 2 * BS), np.float64)
    qdt = np.zeros((NPAIR, 128, BS), np.float64)
    kdt = np.zeros((NPAIR, 128, BS), np.float64)
    for j in range(NPAIR):
        for hh in range(2):
            sl = s[2 * j + hh]
            diff = i[:, None] - i[None, :]
            m = np.where(diff >= 0, np.exp(-sl * np.maximum(diff, 0.0)), 0.0)
            ddt[j, :, hh * BS:(hh + 1) * BS] = m.T
            qdt[j, hh * HD:(hh + 1) * HD, :] = np.exp(-sl * (i + 1.0))[None, :]
            kdt[j, hh * HD:(hh + 1) * HD, :] = np.exp(-sl * (BS - 1.0 - i))[None, :]
    sh["ddt"] = ddt.astype(BF)
    sh["qdt"] = qdt.astype(BF)
    sh["kdt"] = kdt.astype(BF)
    bdv = np.zeros((128, NPAIR), np.float64)
    for j in range(NPAIR):
        for hh in range(2):
            bdv[hh * HD:(hh + 1) * HD, j] = np.exp(-s[2 * j + hh] * BS)
    sh["bd_pp"] = bdv.astype(np.float32)
    sh["ident"] = np.eye(128, dtype=BF)
    sh["ones_col_f"] = np.ones((128, 1), np.float32)
    sh["ones_col_b"] = np.ones((128, 1), BF)
    sh["ones_row_b"] = np.ones((1, 4), BF)
    sh["ones_row_f4"] = np.ones((1, 4), np.float32)
    sh["ones_row_f"] = np.ones((1, 128), np.float32)

    per_core = []
    for c in range(8):
        b, h = c // 2, c % 2
        d = {}
        xs = x[b, h * TPC:(h + 1) * TPC, :]
        d["x_img"] = _img(np.ascontiguousarray(xs.T))
        d["wagg"] = _img(np.ascontiguousarray(agg_w[:, c * CSL:(c + 1) * CSL]))
        d["wbc"] = _img(np.ascontiguousarray(bc_w[:, c * CSL:(c + 1) * CSL]))
        d["wd1"] = _img(np.ascontiguousarray(d1_w[:, c * SLC:(c + 1) * SLC]))
        d["wd2"] = _img(np.ascontiguousarray(d2_w[c * SLC:(c + 1) * SLC, :]))
        d["aggb_row"] = np.ascontiguousarray(
            agg_b[None, c * CSL:(c + 1) * CSL]).astype(BF)
        d["bcb_row"] = np.ascontiguousarray(
            bc_b[None, c * CSL:(c + 1) * CSL]).astype(BF)
        d["d1b_row"] = np.ascontiguousarray(
            d1_b[None, c * SLC:(c + 1) * SLC]).astype(BF)
        d["recvmask"] = np.full((128, 1), 1.0 if h == 1 else 0.0, np.float32)
        d["sendmask"] = np.full((128, 1), 1.0 if h == 0 else 0.0, np.float32)
        bsel = np.zeros((4, 1), np.float32)
        bsel[b, 0] = 1.0
        d["bsel"] = bsel
        per_core.append(d)
    return sh, per_core



class Sem:
    def __init__(self, handle, name):
        self.h = handle
        self.name = name
        self.n = 0


class Prog:
    def __init__(self):
        self.ops = {e: [] for e in ENGS}

    def on(self, engname, fn):
        self.ops[engname].append(fn)

    def run(self, blk):
        for e in ENGS:
            fns = self.ops[e]
            getattr(blk, e)(lambda eng, fns=fns: [f(eng) for f in fns])



def build_nc(stop_after="ALL"):
    nc = bass.Bass("TRN2", target_bir_lowering=False, debug=False)
    ap = {}

    def EIN(name, shape, dt=F32):
        ap[name] = nc.dram_tensor(name, list(shape), dt, kind="ExternalInput").ap()

    EIN("x_img", (128, NCH * TPC))
    EIN("wq", (NPAIR, 128, NCH * 128))
    EIN("wk", (NPAIR, 128, NCH * 128))
    EIN("wv", (NPAIR, 128, NCH * 128))
    EIN("wga", (NCH, 128, NCH * 128))
    EIN("wo", (NCH, 128, NCH * 128))
    EIN("wgf", (128, NCH * DM))
    EIN("wagg", (128, NCH * CSL))
    EIN("wbc", (128, NCH * CSL))
    EIN("wd1", (128, NCH * SLC))
    EIN("wd2", (128, (SLC // 128) * DM))
    EIN("aggb_row", (1, CSL), BF16)
    EIN("bcb_row", (1, CSL), BF16)
    EIN("d1b_row", (1, SLC), BF16)
    EIN("d2b_row", (1, DM))
    for nm in ("normw_pp", "ing_pp", "inb_pp", "png_pp", "pnb_pp",
               "outg_pp", "outb_pp", "gfb_pp"):
        EIN(nm, (128, NCH))
    EIN("xlast4", (4, DM))
    EIN("ddt", (NPAIR, BS, 2 * BS), BF16)
    EIN("qdt", (NPAIR, 128, BS), BF16)
    EIN("kdt", (NPAIR, 128, BS), BF16)
    EIN("bd_pp", (128, NPAIR))
    EIN("recvmask", (128, 1))
    EIN("ident", (128, 128), BF16)
    EIN("ones_col_f", (128, 1))
    EIN("ones_col_b", (128, 1), BF16)
    EIN("ones_row_b", (1, 4), BF16)
    EIN("ones_row_f4", (1, 4))
    EIN("ones_row_f", (1, 128))
    EIN("sendmask", (128, 1))
    EIN("bsel", (4, 1))

    out_img = nc.dram_tensor("out_img", [128, NCH * TPC], F32,
                             kind="ExternalOutput").ap()
    yres = nc.dram_tensor("yres", [128, NCH * TPC], F32).ap()
    zres = nc.dram_tensor("zres", [128, NCH * TPC], F32).ap()
    agg_in = nc.dram_tensor("agg_in", [4, CSL], F32).ap()
    agg_out = nc.dram_tensor("agg_out", [8 * 4, CSL], F32).ap()
    bcx_in = nc.dram_tensor("bcx_in", [4, CSL], F32).ap()
    bcx_out = nc.dram_tensor("bcx_out", [8 * 4, CSL], F32).ap()
    ar_in = [nc.dram_tensor(f"ar_in{i}", [4, DM], F32).ap() for i in range(8)]
    ar_out = [nc.dram_tensor(f"ar_out{i}", [4, DM], F32).ap() for i in range(8)]
    st_in = [nc.dram_tensor(f"st_in{j}", [128, HD], F32).ap() for j in range(NPAIR)]
    st_out = [nc.dram_tensor(f"st_out{j}", [128, HD], F32).ap() for j in range(NPAIR)]
    zl_in = nc.dram_tensor("zl_in", [1, DM], F32).ap()
    zl_out = nc.dram_tensor("zl_out", [8, DM], F32).ap()

    dbg_taps = {}
    if stop_after in ("P1", "P2"):
        pass

    import contextlib
    with contextlib.ExitStack() as _stack:
        big_a = _stack.enter_context(nc.sbuf_tensor("big_a", [128, 32768], BF16))
        big_b = _stack.enter_context(nc.sbuf_tensor("big_b", [128, 32768], BF16))
        wpool = _stack.enter_context(nc.sbuf_tensor("wpool", [128, 8192], BF16))
        scratch = _stack.enter_context(nc.sbuf_tensor("scratch", [128, 8192], F32))
        parena = _stack.enter_context(nc.sbuf_tensor("parena", [128, 9728], BF16))
        wstage = _stack.enter_context(nc.sbuf_tensor("wstage", [128, 1024], F32))
        miscf = _stack.enter_context(nc.sbuf_tensor("miscf", [128, 944], F32))
        dect = _stack.enter_context(nc.sbuf_tensor("dect", [128, 640], BF16))
        tmpb = _stack.enter_context(nc.sbuf_tensor("tmpb", [128, 960], BF16))
        cst_f = _stack.enter_context(nc.sbuf_tensor("cst_f", [128, 2 + NCH * 10 + NPAIR + 2], F32))
        cst_b = _stack.enter_context(nc.sbuf_tensor("cst_b", [128, 129], BF16))
        o4 = _stack.enter_context(nc.sbuf_tensor("o4", [1, 4], BF16))
        bselt = _stack.enter_context(nc.sbuf_tensor("bselt", [4, 1], F32))
        ps0 = _stack.enter_context(nc.psum_tensor("ps0", [128, 512], F32))
        ps1 = _stack.enter_context(nc.psum_tensor("ps1", [128, 512], F32))
        ps2 = _stack.enter_context(nc.psum_tensor("ps2", [128, 512], F32))
        ps3 = _stack.enter_context(nc.psum_tensor("ps3", [128, 512], F32))
        ps4 = _stack.enter_context(nc.psum_tensor("ps4", [128, 512], F32))
        ps5 = _stack.enter_context(nc.psum_tensor("ps5", [128, 512], F32))
        pt0 = _stack.enter_context(nc.psum_tensor("pt0", [128, 1024], BF16))
        ps6 = _stack.enter_context(nc.psum_tensor("ps6", [128, 512], F32))
        _dA = _stack.enter_context(nc.semaphore("dA"))
        _dW0 = _stack.enter_context(nc.semaphore("dW0"))
        _dW1 = _stack.enter_context(nc.semaphore("dW1"))
        _dB = _stack.enter_context(nc.semaphore("dB"))
        _dC = _stack.enter_context(nc.semaphore("dC"))
        _dG = _stack.enter_context(nc.semaphore("dG"))
        _dR0 = _stack.enter_context(nc.semaphore("dR0"))
        _dR1 = _stack.enter_context(nc.semaphore("dR1"))
        _cC = _stack.enter_context(nc.semaphore("cC"))
        _tT = _stack.enter_context(nc.semaphore("tT"))
        _vV = _stack.enter_context(nc.semaphore("vV"))
        _sS = _stack.enter_context(nc.semaphore("sS"))
        _vW = _stack.enter_context(nc.semaphore("vW"))
        blk = _stack.enter_context(nc.Block())
        dA = Sem(_dA, "dA")
        dW = [Sem(_dW0, "dW0"), Sem(_dW1, "dW1")]
        dB = Sem(_dB, "dB")
        dC = Sem(_dC, "dC")
        dG = Sem(_dG, "dG")
        dR = [Sem(_dR0, "dR0"), Sem(_dR1, "dR1")]
        cC = Sem(_cC, "cC")
        tT = Sem(_tT, "tT")
        vV = Sem(_vV, "vV")
        sS = Sem(_sS, "sS")
        vW = Sem(_vW, "vW")
        P = Prog()

        pp_names = ["normw_pp", "ing_pp", "inb_pp", "png_pp", "pnb_pp",
                    "outg_pp", "outb_pp", "gfb_pp", "outg_half", "outb_half"]
        PPOF = {n: 2 + NCH * i for i, n in enumerate(pp_names)}
        BDOF = 2 + NCH * 10
        SMOF = BDOF + NPAIR
        RMOF = SMOF + 1

        BC1 = 0
        BC2 = 16
        RNEG, RVAR = 32, 288
        ONESR = 544
        SR4 = 672
        UTMP = 688

        ffnpre = scratch[:, 0:4096]
        xsec = scratch[:, 4096:8192]
        T_tmp = scratch[0:4, 0:2048]
        T_work = scratch[0:4, 2048:4096]
        T_bm = scratch[0:4, 6144:8192]
        statef = scratch[:, 0:320]
        bc_rowt = scratch[0:1, 4096:6144]
        d2b_row = scratch[0:1, 4096:6144]
        onesr = miscf[0:1, ONESR:ONESR + 128]
        o4f = miscf[0:1, ONESR:ONESR + 4]
        srow = miscf[0:4, SR4:SR4 + 16]
        utmp = miscf[:, UTMP:UTMP + 256]

        wagg_b = parena[:, 0:NCH * CSL]
        wbc_b = parena[:, NCH * CSL:2 * NCH * CSL]
        aggb_v = parena[0:1, 8192:8192 + CSL]
        bcb_v = parena[0:1, 8448:8448 + CSL]
        d1b_v = parena[0:1, 8704:8704 + SLC]
        wd1_b = big_b[:, 0:NCH * SLC]
        wd2_b = big_b[:, NCH * SLC:NCH * SLC + (SLC // 128) * DM]

        u1sec = wpool[:, 0:4096]
        sqt = [wpool[:, 4096 + i * 256:4096 + (i + 1) * 256] for i in range(2)]
        gat = [wpool[:, 4608 + i * 256:4608 + (i + 1) * 256] for i in range(2)]
        T_rowb = wpool[0:4, 5120:7168]
        T_fm = wpool[:, 7168:7232]
        g_fm = wpool[:, 7232:7264]

        def const_loads():
            def f(eng):
                eng.dma_start(out=cst_f[:, 0:1], in_=ap["ones_col_f"][:, :]).then_inc(_dA, 16)
                for n in pp_names[:8]:
                    eng.dma_start(out=cst_f[:, PPOF[n]:PPOF[n] + NCH],
                                  in_=ap[n][:, :]).then_inc(_dA, 16)
                eng.dma_start(out=cst_f[:, BDOF:BDOF + NPAIR],
                              in_=ap["bd_pp"][:, :]).then_inc(_dA, 16)
                eng.dma_start(out=cst_f[:, RMOF:RMOF + 1],
                              in_=ap["recvmask"][:, :]).then_inc(_dA, 16)
                eng.dma_start(out=cst_f[:, SMOF:SMOF + 1],
                              in_=ap["sendmask"][:, :]).then_inc(_dA, 16)
                eng.dma_start(out=cst_b[:, 0:128], in_=ap["ident"][:, :]).then_inc(_dA, 16)
                eng.dma_start(out=cst_b[:, 128:129], in_=ap["ones_col_b"][:, :]).then_inc(_dA, 16)
                eng.dma_start(out=o4[:, :], in_=ap["ones_row_b"][:, :]).then_inc(_dA, 16)
                eng.dma_start(out=onesr, in_=ap["ones_row_f"][:, :]).then_inc(_dA, 16)
                eng.dma_start(out=T_bm, in_=ap["xlast4"][:, :]).then_inc(_dA, 16)
                eng.dma_start(out=bselt[:, :], in_=ap["bsel"][:, :]).then_inc(_dA, 16)
            P.on("sync", f)
            dA.n += 16 * 18

            w0 = dA.n

            def g(eng, w0=w0):
                eng.wait_ge(_dA, w0)
                eng.tensor_scalar(out=cst_f[:, PPOF["outg_half"]:PPOF["outg_half"] + NCH],
                                  in0=cst_f[:, PPOF["outg_pp"]:PPOF["outg_pp"] + NCH],
                                  scalar1=0.5, scalar2=None, op0=ALU.mult)
                ins = eng.tensor_scalar(out=cst_f[:, PPOF["outb_half"]:PPOF["outb_half"] + NCH],
                                        in0=cst_f[:, PPOF["outb_pp"]:PPOF["outb_pp"] + NCH],
                                        scalar1=0.5, scalar2=None, op0=ALU.mult)
                ins.then_inc(_vV, 1)
            P.on("vector", g)
            vV.n += 1

        const_loads()
        CONST_DA = dA.n
        CONST_VV = vV.n

        def load_thought_rows(tag):
            war_v = vV.n
            war_c = dC.n

            def f(eng, war_v=war_v, war_c=war_c):
                eng.wait_ge(_vV, war_v)
                eng.wait_ge(_dC, war_c)
                eng.dma_start(out=d2b_row, in_=ap["d2b_row"][:, :]).then_inc(_dA, 16)
                eng.dma_start(out=aggb_v, in_=ap["aggb_row"][:, :]).then_inc(_dA, 16)
                eng.dma_start(out=bcb_v, in_=ap["bcb_row"][:, :]).then_inc(_dA, 16)
                eng.dma_start(out=d1b_v, in_=ap["d1b_row"][:, :]).then_inc(_dA, 16)
            P.on("sync", f)
            dA.n += 16 * 4
            w1 = dA.n

            def g(eng, w1=w1):
                eng.wait_ge(_dA, w1)
                ins = eng.tensor_scalar(out=d2b_row, in0=d2b_row,
                                        scalar1=0.125, scalar2=None, op0=ALU.mult)
                ins.then_inc(_vV, 1)
            P.on("vector", g)
            vV.n += 1
            return vV.n

        def stream_cast(src_ap, src_lo, home, home_lo, width):
            hw = tT.n
            hv = vV.n
            hs = sS.n
            for p in range((width + 511) // 512):
                lo = p * 512
                w = min(512, width - lo)
                slot = p % 2
                war = max(0, vW.n - 1)

                def f(eng, lo=lo, w=w, slot=slot, war=war):
                    eng.wait_ge(_vW, war)
                    eng.dma_start(out=wstage[:, slot * 512:slot * 512 + w],
                                  in_=src_ap[:, src_lo + lo:src_lo + lo + w]
                                  ).then_inc(dW[slot].h, 16)
                P.on("sync", f)
                dW[slot].n += 16
                w1 = dW[slot].n

                def g(eng, lo=lo, w=w, slot=slot, w1=w1, hw=hw, hv=hv, hs=hs):
                    eng.wait_ge(dW[slot].h, w1)
                    eng.wait_ge(_tT, hw)
                    eng.wait_ge(_vV, hv)
                    eng.wait_ge(_sS, hs)
                    ins = eng.tensor_copy(home[:, home_lo + lo:home_lo + lo + w],
                                          wstage[:, slot * 512:slot * 512 + w])
                    ins.then_inc(_vW, 1)
                P.on("vector", g)
                vW.n += 1
            return vW.n

        def row_ln(dst, src, eps, waits):
            s1 = srow[:, 0:1]; s2 = srow[:, 1:2]; negm = srow[:, 2:3]
            s2e = srow[:, 3:4]; msq = srow[:, 4:5]; sv = srow[:, 5:6]
            rs = srow[:, 6:7]

            def v1(eng, waits=list(waits)):
                for s_, v_ in waits:
                    eng.wait_ge(s_.h, v_)
                ins = eng.tensor_reduce(out=s1, in_=src, axis=AX.X, op=ALU.add)
                ins.then_inc(_vV, 1)
            P.on("vector", v1)
            vV.n += 1

            def sq1(eng, waits=list(waits)):
                for s_, v_ in waits:
                    eng.wait_ge(s_.h, v_)
                ins = eng.square(T_work, src)
                ins.then_inc(_sS, 1)
            P.on("scalar", sq1)
            sS.n += 1
            wsq = sS.n

            def v2(eng, wsq=wsq):
                eng.wait_ge(_sS, wsq)
                ins = eng.tensor_reduce(out=s2, in_=T_work, axis=AX.X, op=ALU.add)
                ins.then_inc(_vV, 1)
            P.on("vector", v2)
            vV.n += 1
            wred = vV.n

            def s2f(eng, wred=wred):
                eng.wait_ge(_vV, wred)
                eng.activation(out=negm, in_=s1, func=AF.Copy, scale=-1.0 / DM)
                ins = eng.activation(out=s2e, in_=s2, func=AF.Copy,
                                     scale=1.0 / DM, bias=float(eps))
                ins.then_inc(_sS, 1)
            P.on("scalar", s2f)
            sS.n += 1
            wm = sS.n

            def v3(eng, wm=wm):
                eng.wait_ge(_sS, wm)
                ins = eng.tensor_tensor(out=msq, in0=negm, in1=negm, op=ALU.mult)
                ins.then_inc(_vV, 1)
            P.on("vector", v3)
            vV.n += 1
            wmsq = vV.n

            def s3f(eng, wmsq=wmsq):
                eng.wait_ge(_vV, wmsq)
                ins = eng.activation(out=sv, in_=msq, func=AF.Sqrt,
                                     scale=-1.0, bias=s2e)
                ins.then_inc(_sS, 1)
            P.on("scalar", s3f)
            sS.n += 1
            wsv = sS.n

            def v4(eng, wsv=wsv):
                eng.wait_ge(_sS, wsv)
                ins = eng.reciprocal(out=rs, in_=sv)
                ins.then_inc(_vV, 1)
            P.on("vector", v4)
            vV.n += 1
            wrs = vV.n

            def v5(eng, wrs=wrs):
                eng.wait_ge(_vV, wrs)
                ins = eng.tensor_scalar(out=dst, in0=src, scalar1=negm,
                                        scalar2=rs, op0=ALU.add, op1=ALU.mult)
                ins.then_inc(_vV, 1)
            P.on("vector", v5)
            vV.n += 1

        def rows_to_fm(dst_fm, src_rows, nch128, gcol=None, bcol=None):
            wsrc = vV.n

            def v1(eng, wsrc=wsrc):
                eng.wait_ge(_vV, wsrc)
                ins = eng.tensor_copy(T_rowb[:, 0:nch128 * 128], src_rows)
                ins.then_inc(_vV, 1)
            P.on("vector", v1)
            vV.n += 1
            wcast = vV.n

            def t1(eng, wcast=wcast):
                eng.wait_ge(_vV, wcast)
                for c in range(nch128):
                    ins = eng.transpose(out=pt0[:, c * 4:(c + 1) * 4],
                                        in_=T_rowb[0:4, c * 128:(c + 1) * 128],
                                        identity=cst_b[0:4, 0:4])
                ins.then_inc(_tT, 1)
            P.on("tensor", t1)
            tT.n += 1
            wtr = tT.n

            def v2(eng, wtr=wtr):
                eng.wait_ge(_tT, wtr)
                if gcol is None:
                    ins = eng.tensor_copy(dst_fm[:, 0:nch128 * 4], pt0[:, 0:nch128 * 4])
                else:
                    for c in range(nch128):
                        ins = eng.tensor_scalar(
                            out=dst_fm[:, c * 4:(c + 1) * 4],
                            in0=pt0[:, c * 4:(c + 1) * 4],
                            scalar1=cst_f[:, gcol + c:gcol + c + 1],
                            scalar2=cst_f[:, bcol + c:bcol + c + 1],
                            op0=ALU.mult, op1=ALU.add)
                ins.then_inc(_vV, 1)
            P.on("vector", v2)
            vV.n += 1

        def thought(tag, bc_col, wready_vw):
            ar_i = ar_in[(tag - 1) * 4:(tag - 1) * 4 + 4]
            ar_o = ar_out[(tag - 1) * 4:(tag - 1) * 4 + 4]

            if tag == 2:
                wz = dC.n

                def gz(eng, wz=wz, g0=dG.n, c0=cC.n):
                    eng.wait_ge(_dC, wz)
                    eng.collective_compute(
                        "AllGather", ALU.bypass, replica_groups=[list(range(8))],
                        ins=[zl_in.opt()], outs=[zl_out.opt()]).then_inc(_cC, 1)
                    eng.wait_ge(_cC, c0 + 1)
                    eng.dma_start(out=T_bm,
                                  in_=zl_out.rearrange("(b o) d -> b o d", o=2)[:, 1, :]
                                  ).then_inc(_dG, 16)
                P.on("gpsimd", gz)
                dG.n += 16
                cC.n += 1
                w0 = [(dG, dG.n)]
            else:
                w0 = [(dA, CONST_DA)]

            row_ln(T_tmp, T_bm, EPS_OUT, w0)
            row_ln(T_bm, T_tmp, EPS_FFN, [(vV, vV.n)])
            rows_to_fm(T_fm, T_bm[:, 0:2048], NCH,
                       gcol=PPOF["ing_pp"], bcol=PPOF["inb_pp"])

            wfm = vV.n

            def t_agg(eng, wfm=wfm, wready=wready_vw, wda=dA.n, wss=sS.n):
                eng.wait_ge(_vV, wfm)
                eng.wait_ge(_vW, wready)
                eng.wait_ge(_dA, wda)
                eng.wait_ge(_sS, wss)
                for c in range(NCH):
                    eng.matmul(ps0[0:4, 0:CSL], T_fm[:, c * 4:(c + 1) * 4],
                               wagg_b[:, c * CSL:(c + 1) * CSL],
                               start=(c == 0), stop=False)
                ins = eng.matmul(ps0[0:4, 0:CSL], o4[:, :], aggb_v,
                                 start=False, stop=True)
                ins.then_inc(_tT, 1)
            P.on("tensor", t_agg)
            tT.n += 1
            wg = tT.n

            def v_agg(eng, wg=wg):
                eng.wait_ge(_tT, wg)
                ins = eng.tensor_copy(T_tmp[:, 0:CSL], ps0[0:4, 0:CSL])
                ins.then_inc(_vV, 1)
            P.on("vector", v_agg)
            vV.n += 1
            wv = vV.n

            def g_agg(eng, wv=wv, g0=dG.n, c0=cC.n):
                eng.wait_ge(_vV, wv)
                eng.dma_start(out=agg_in[:, :], in_=T_tmp[:, 0:CSL]).then_inc(_dG, 16)
                eng.wait_ge(_dG, g0 + 16)
                eng.collective_compute(
                    "AllGather", ALU.bypass, replica_groups=[list(range(8))],
                    ins=[agg_in.opt()], outs=[agg_out.opt()]).then_inc(_cC, 1)
                eng.wait_ge(_cC, c0 + 1)
                eng.dma_start(out=T_bm.rearrange("b (r n) -> b r n", r=8),
                              in_=agg_out.rearrange("(r b) n -> b r n", b=4)
                              ).then_inc(_dG, 16)
            P.on("gpsimd", g_agg)
            dG.n += 32
            cC.n += 1

            for ks in range(KSTEPS):
                row_ln(T_tmp, T_bm, EPS_FFN,
                       [(dG, dG.n)] if ks == 0 else [(vV, vV.n)])
                rows_to_fm(T_fm, T_tmp[:, 0:2048], NCH,
                           gcol=PPOF["png_pp"], bcol=PPOF["pnb_pp"])
                wfm2 = vV.n

                def t_d1(eng, wfm2=wfm2):
                    eng.wait_ge(_vV, wfm2)
                    for half, ps in ((0, ps0), (1, ps1)):
                        for c in range(NCH):
                            eng.matmul(ps[0:4, :], T_fm[:, c * 4:(c + 1) * 4],
                                       wd1_b[:, c * SLC + half * 512:c * SLC + (half + 1) * 512],
                                       start=(c == 0), stop=False)
                        ins = eng.matmul(ps[0:4, :], o4[:, :],
                                         d1b_v[:, half * 512:(half + 1) * 512],
                                         start=False, stop=True)
                    ins.then_inc(_tT, 1)
                P.on("tensor", t_d1)
                tT.n += 1
                wd1t = tT.n

                def s_gelu(eng, wd1t=wd1t):
                    eng.wait_ge(_tT, wd1t)
                    eng.activation(out=T_rowb[:, 0:512], in_=ps0[0:4, :], func=AF.Gelu)
                    ins = eng.activation(out=T_rowb[:, 512:1024], in_=ps1[0:4, :], func=AF.Gelu)
                    ins.then_inc(_sS, 1)
                P.on("scalar", s_gelu)
                sS.n += 1
                wge = sS.n

                def t_trg(eng, wge=wge):
                    eng.wait_ge(_sS, wge)
                    for c in range(8):
                        ins = eng.transpose(out=pt0[:, c * 4:(c + 1) * 4],
                                            in_=T_rowb[0:4, c * 128:(c + 1) * 128],
                                            identity=cst_b[0:4, 0:4])
                    ins.then_inc(_tT, 1)
                P.on("tensor", t_trg)
                tT.n += 1
                wtg = tT.n

                def v_g(eng, wtg=wtg):
                    eng.wait_ge(_tT, wtg)
                    ins = eng.tensor_copy(g_fm[:, 0:32], pt0[:, 0:32])
                    ins.then_inc(_vV, 1)
                P.on("vector", v_g)
                vV.n += 1
                wgf = vV.n

                def t_d2(eng, wgf=wgf):
                    eng.wait_ge(_vV, wgf)
                    for q, ps in enumerate((ps0, ps1, ps2, ps3)):
                        for c in range(8):
                            eng.matmul(ps[0:4, :], g_fm[:, c * 4:(c + 1) * 4],
                                       wd2_b[:, c * DM + q * 512:c * DM + (q + 1) * 512],
                                       start=(c == 0), stop=False)
                    for q, ps in enumerate((ps0, ps1, ps2, ps3)):
                        ins = eng.matmul(ps[0:4, :], o4f[:, :],
                                         d2b_row[:, q * 512:(q + 1) * 512],
                                         start=False, stop=True)
                    ins.then_inc(_tT, 1)
                P.on("tensor", t_d2)
                tT.n += 1
                wd2t = tT.n

                def v_d2(eng, wd2t=wd2t):
                    eng.wait_ge(_tT, wd2t)
                    for q, ps in enumerate((ps0, ps1, ps2, ps3)):
                        ins = eng.tensor_copy(T_work[:, q * 512:(q + 1) * 512], ps[0:4, :])
                    ins.then_inc(_vV, 1)
                P.on("vector", v_d2)
                vV.n += 1
                wpar = vV.n
                ari, aro = ar_i[ks], ar_o[ks]

                def g_ar(eng, wpar=wpar, ari=ari, aro=aro, g0=dG.n, c0=cC.n):
                    eng.wait_ge(_vV, wpar)
                    eng.dma_start(out=ari[:, :], in_=T_work).then_inc(_dG, 16)
                    eng.wait_ge(_dG, g0 + 16)
                    eng.collective_compute(
                        "AllReduce", ALU.add, replica_groups=[list(range(8))],
                        ins=[ari.opt()], outs=[aro.opt()]).then_inc(_cC, 1)
                    eng.wait_ge(_cC, c0 + 1)
                    eng.dma_start(out=T_tmp, in_=aro[:, :]).then_inc(_dG, 16)
                P.on("gpsimd", g_ar)
                dG.n += 32
                cC.n += 1
                war = dG.n

                def v_acc(eng, war=war):
                    eng.wait_ge(_dG, war)
                    ins = eng.tensor_tensor(out=T_bm, in0=T_bm, in1=T_tmp, op=ALU.add)
                    ins.then_inc(_vV, 1)
                P.on("vector", v_acc)
                vV.n += 1

            rows_to_fm(T_fm, T_bm[:, 0:2048], NCH)
            wfm3 = vV.n

            def t_bc(eng, wfm3=wfm3):
                eng.wait_ge(_vV, wfm3)
                for c in range(NCH):
                    eng.matmul(ps0[0:4, 0:CSL], T_fm[:, c * 4:(c + 1) * 4],
                               wbc_b[:, c * CSL:(c + 1) * CSL],
                               start=(c == 0), stop=False)
                ins = eng.matmul(ps0[0:4, 0:CSL], o4[:, :], bcb_v,
                                 start=False, stop=True)
                ins.then_inc(_tT, 1)
            P.on("tensor", t_bc)
            tT.n += 1
            wbcp = tT.n

            def v_bc(eng, wbcp=wbcp):
                eng.wait_ge(_tT, wbcp)
                ins = eng.tensor_copy(T_tmp[:, 0:CSL], ps0[0:4, 0:CSL])
                ins.then_inc(_vV, 1)
            P.on("vector", v_bc)
            vV.n += 1
            wv2 = vV.n

            def g_bc(eng, wv2=wv2, g0=dG.n, c0=cC.n):
                eng.wait_ge(_vV, wv2)
                eng.dma_start(out=bcx_in[:, :], in_=T_tmp[:, 0:CSL]).then_inc(_dG, 16)
                eng.wait_ge(_dG, g0 + 16)
                eng.collective_compute(
                    "AllGather", ALU.bypass, replica_groups=[list(range(8))],
                    ins=[bcx_in.opt()], outs=[bcx_out.opt()]).then_inc(_cC, 1)
                eng.wait_ge(_cC, c0 + 1)
                eng.dma_start(out=T_work.rearrange("b (r n) -> b r n", r=8),
                              in_=bcx_out.rearrange("(r b) n -> b r n", b=4)
                              ).then_inc(_dG, 16)
            P.on("gpsimd", g_bc)
            dG.n += 32
            cC.n += 1
            wbg = dG.n

            prev_v = 0
            for q in range(4):
                def t_s(eng, q=q, wbg=wbg, prev_v=prev_v):
                    eng.wait_ge(_dG, wbg)
                    if prev_v:
                        eng.wait_ge(_vV, prev_v)
                    ins = eng.matmul(ps1[0:1, 0:512], bselt[:, :],
                                     T_work[:, q * 512:(q + 1) * 512],
                                     start=True, stop=True)
                    ins.then_inc(_tT, 1)
                P.on("tensor", t_s)
                tT.n += 1
                wts = tT.n

                def v_s(eng, q=q, wts=wts):
                    eng.wait_ge(_tT, wts)
                    ins = eng.tensor_copy(bc_rowt[:, q * 512:(q + 1) * 512],
                                          ps1[0:1, 0:512])
                    ins.then_inc(_vV, 1)
                P.on("vector", v_s)
                vV.n += 1
                prev_v = vV.n
            wrow = vV.n

            def t_tr(eng, wrow=wrow):
                eng.wait_ge(_vV, wrow)
                for c in range(NCH):
                    ins = eng.transpose(out=ps2[:, c:c + 1],
                                        in_=bc_rowt[0:1, c * 128:(c + 1) * 128],
                                        identity=cst_f[0:1, 0:1])
                ins.then_inc(_tT, 1)
            P.on("tensor", t_tr)
            tT.n += 1
            wtr2 = tT.n

            def v_fin(eng, wtr2=wtr2, bc_col=bc_col):
                eng.wait_ge(_tT, wtr2)
                ins = eng.tensor_copy(miscf[:, bc_col:bc_col + NCH], ps2[:, 0:NCH])
                ins.then_inc(_vV, 1)
            P.on("vector", v_fin)
            vV.n += 1
            return vV.n

        def sec_stats(src_tiles, eps, sub_mean):
            base_t = tT.n
            base_s = sS.n
            base_v = vV.n

            def s_sq2(eng, base_t=base_t):
                for c in range(NCH):
                    eng.wait_ge(_tT, base_t + c - 1 if c >= 2 else base_t)
                    ins = eng.activation(out=sqt[c % 2], in_=src_tiles(c),
                                         func=AF.Square)
                    ins.then_inc(_sS, 1)
            P.on("scalar", s_sq2)
            sS.n += NCH

            def t_stat(eng, base_s=base_s, base_v=base_v):
                eng.wait_ge(_vV, base_v)
                for c in range(NCH):
                    eng.wait_ge(_sS, base_s + c + 1)
                    ins = eng.matmul(ps5[0:1, 0:SEC], cst_b[:, 128:129],
                                     sqt[c % 2],
                                     start=(c == 0), stop=(c == NCH - 1))
                    ins.then_inc(_tT, 1)
                if sub_mean:
                    for c in range(NCH):
                        ins = eng.matmul(ps4[0:1, 0:SEC], cst_f[:, 0:1],
                                         src_tiles(c),
                                         start=(c == 0), stop=(c == NCH - 1))
                    ins.then_inc(_tT, 1)
            P.on("tensor", t_stat)
            tT.n += NCH + (1 if sub_mean else 0)
            wsum = tT.n

            negm_r = miscf[0:1, RNEG:RNEG + SEC]
            s2e_r = miscf[0:1, RVAR:RVAR + SEC]
            msq_r = miscf[0:1, UTMP:UTMP + SEC]
            rsr = miscf[0:1, RVAR:RVAR + SEC]

            if sub_mean:
                def v_m(eng, wsum=wsum):
                    eng.wait_ge(_tT, wsum)
                    eng.tensor_scalar(out=negm_r, in0=ps4[0:1, 0:SEC],
                                      scalar1=-1.0 / DM, scalar2=None, op0=ALU.mult)
                    eng.tensor_scalar(out=s2e_r, in0=ps5[0:1, 0:SEC],
                                      scalar1=1.0 / DM, scalar2=float(eps),
                                      op0=ALU.mult, op1=ALU.add)
                    eng.tensor_tensor(out=msq_r, in0=negm_r, in1=negm_r, op=ALU.mult)
                    ins = eng.tensor_tensor(out=s2e_r, in0=s2e_r, in1=msq_r,
                                            op=ALU.subtract)
                    ins.then_inc(_vV, 1)
                P.on("vector", v_m)
                vV.n += 1
                wm1 = vV.n

                def s_rs(eng, wm1=wm1):
                    eng.wait_ge(_vV, wm1)
                    ins = eng.activation(out=msq_r, in_=s2e_r, func=AF.Sqrt)
                    ins.then_inc(_sS, 1)
                P.on("scalar", s_rs)
                sS.n += 1
                wsv1 = sS.n

                def v_rs(eng, wsv1=wsv1):
                    eng.wait_ge(_sS, wsv1)
                    ins = eng.reciprocal(out=rsr, in_=msq_r)
                    ins.then_inc(_vV, 1)
                P.on("vector", v_rs)
                vV.n += 1
            else:
                def s_rs2(eng, wsum=wsum):
                    eng.wait_ge(_tT, wsum)
                    ins = eng.activation(out=msq_r, in_=ps5[0:1, 0:SEC],
                                         func=AF.Sqrt, scale=1.0 / DM,
                                         bias=float(eps))
                    ins.then_inc(_sS, 1)
                P.on("scalar", s_rs2)
                sS.n += 1
                wsv2 = sS.n

                def v_rs2(eng, wsv2=wsv2):
                    eng.wait_ge(_sS, wsv2)
                    ins = eng.reciprocal(out=rsr, in_=msq_r)
                    ins.then_inc(_vV, 1)
                P.on("vector", v_rs2)
                vV.n += 1
            wrs = vV.n

            def t_bc2(eng, wrs=wrs):
                eng.wait_ge(_vV, wrs)
                if sub_mean:
                    for hq in range(SEC // 128):
                        eng.matmul(ps4[:, hq * 128:(hq + 1) * 128],
                                   onesr,
                                   negm_r[:, hq * 128:(hq + 1) * 128],
                                   start=True, stop=True)
                for hq in range(SEC // 128):
                    ins = eng.matmul(ps5[:, hq * 128:(hq + 1) * 128],
                                     onesr,
                                     rsr[:, hq * 128:(hq + 1) * 128],
                                     start=True, stop=True)
                ins.then_inc(_tT, 1)
            P.on("tensor", t_bc2)
            tT.n += 1
            return tT.n

        def emit_ffn_sections(tag, src_dram, dst_dram, bc_col, wgf_vw, bc_vv):
            for s_i in range(NSEC):
                t0 = s_i * SEC
                war_v = vV.n
                war_c = dC.n

                def f_in(eng, t0=t0, war_v=war_v, war_c=war_c):
                    eng.wait_ge(_vV, war_v)
                    eng.wait_ge(_dC, war_c)
                    eng.dma_start(
                        out=xsec.rearrange("p (c t) -> p c t", c=NCH),
                        in_=src_dram.rearrange("p (c t) -> p c t", c=NCH)[:, :, t0:t0 + SEC]
                    ).then_inc(_dB, 16)
                P.on("sync", f_in)
                dB.n += 16
                wxin = dB.n

                for e in ("tensor", "scalar", "vector"):
                    def guard(eng, wxin=wxin):
                        eng.wait_ge(_dB, wxin)
                    P.on(e, guard)

                wbc1 = sec_stats(lambda c: xsec[:, c * SEC:(c + 1) * SEC],
                                 EPS_OUT, True)

                def v_ap(eng, wbc1=wbc1):
                    eng.wait_ge(_tT, wbc1)
                    for c in range(NCH):
                        eng.tensor_tensor(out=utmp, in0=xsec[:, c * SEC:(c + 1) * SEC],
                                          in1=ps4[:, 0:SEC], op=ALU.add)
                        ins = eng.tensor_tensor(out=u1sec[:, c * SEC:(c + 1) * SEC],
                                                in0=utmp, in1=ps5[:, 0:SEC], op=ALU.mult)
                    ins.then_inc(_vV, 1)
                P.on("vector", v_ap)
                vV.n += 1
                wu1 = vV.n

                base_s2 = sS.n
                base_v2 = vV.n

                def t_gate(eng, wu1=wu1, wgf_vw=wgf_vw, base_s2=base_s2):
                    eng.wait_ge(_vV, wu1)
                    eng.wait_ge(_vW, wgf_vw)
                    for cn in range(NCH):
                        psg = (ps0, ps1)[cn % 2]
                        eng.wait_ge(_sS, base_s2 + cn - 1 if cn >= 2 else base_s2)
                        for ck in range(NCH):
                            ins = eng.matmul(psg[:, 0:SEC],
                                             big_a[:, ck * DM + cn * 128:ck * DM + (cn + 1) * 128],
                                             u1sec[:, ck * SEC:(ck + 1) * SEC],
                                             start=(ck == 0), stop=(ck == NCH - 1))
                        ins.then_inc(_tT, 1)
                P.on("tensor", t_gate)
                tgate = []
                for cn in range(NCH):
                    tT.n += 1
                    tgate.append(tT.n)

                def s_gate(eng, tgate=list(tgate), base_v2=base_v2):
                    for cn in range(NCH):
                        psg = (ps0, ps1)[cn % 2]
                        eng.wait_ge(_tT, tgate[cn])
                        eng.wait_ge(_vV, base_v2 + cn - 1 if cn >= 2 else base_v2)
                        ins = eng.activation(out=gat[cn % 2], in_=psg[:, 0:SEC],
                                             func=AF.Sigmoid,
                                             bias=cst_f[:, PPOF["gfb_pp"] + cn:PPOF["gfb_pp"] + cn + 1])
                        ins.then_inc(_sS, 1)
                P.on("scalar", s_gate)
                sgate = []
                for cn in range(NCH):
                    sS.n += 1
                    sgate.append(sS.n)

                def v_comb(eng, sgate=list(sgate), bc_vv=bc_vv, bc_col=bc_col):
                    eng.wait_ge(_vV, bc_vv)
                    for cn in range(NCH):
                        eng.wait_ge(_sS, sgate[cn])
                        ins = eng.scalar_tensor_tensor(
                            out=ffnpre[:, cn * SEC:(cn + 1) * SEC],
                            in0=gat[cn % 2],
                            scalar=miscf[:, bc_col + cn:bc_col + cn + 1],
                            in1=u1sec[:, cn * SEC:(cn + 1) * SEC],
                            op0=ALU.mult, op1=ALU.add)
                        ins.then_inc(_vV, 1)
                P.on("vector", v_comb)
                for cn in range(NCH):
                    vV.n += 1

                wbc2 = sec_stats(lambda c: ffnpre[:, c * SEC:(c + 1) * SEC],
                                 EPS_FFN, True)

                base_dc = dC.n

                def v_fin(eng, wbc2=wbc2, base_dc=base_dc):
                    eng.wait_ge(_tT, wbc2)
                    eng.wait_ge(_dC, base_dc)
                    for c in range(NCH):
                        eng.tensor_tensor(out=utmp, in0=ffnpre[:, c * SEC:(c + 1) * SEC],
                                          in1=ps4[:, 0:SEC], op=ALU.add)
                        eng.scalar_tensor_tensor(
                            out=utmp, in0=utmp,
                            scalar=cst_f[:, PPOF["outg_half"] + c:PPOF["outg_half"] + c + 1],
                            in1=ps5[:, 0:SEC], op0=ALU.mult, op1=ALU.mult)
                        ins = eng.scalar_tensor_tensor(
                            out=xsec[:, c * SEC:(c + 1) * SEC], in0=utmp,
                            scalar=cst_f[:, PPOF["outb_half"] + c:PPOF["outb_half"] + c + 1],
                            in1=xsec[:, c * SEC:(c + 1) * SEC],
                            op0=ALU.add, op1=ALU.add)
                        ins.then_inc(_vV, 1)
                P.on("vector", v_fin)
                vfin = []
                for c in range(NCH):
                    vV.n += 1
                    vfin.append(vV.n)

                def f_out(eng, vfin=list(vfin), t0=t0):
                    for c in range(NCH):
                        eng.wait_ge(_vV, vfin[c])
                        eng.dma_start(
                            out=dst_dram[:, c * TPC + t0:c * TPC + t0 + SEC],
                            in_=xsec[:, c * SEC:(c + 1) * SEC]
                        ).then_inc(_dC, 16)
                P.on("sync", f_out)
                dC.n += 16 * NCH

        def emit_u2():
            war_t = tT.n

            def v_guard(eng, war_t=war_t):
                eng.wait_ge(_tT, war_t)
            P.on("vector", v_guard)

            for s_i in range(NSEC):
                t0 = s_i * SEC
                war_v = vV.n
                war_c = dC.n

                def f_in(eng, t0=t0, war_v=war_v, war_c=war_c):
                    eng.wait_ge(_vV, war_v)
                    eng.wait_ge(_dC, war_c)
                    eng.dma_start(
                        out=xsec.rearrange("p (c t) -> p c t", c=NCH),
                        in_=yres.rearrange("p (c t) -> p c t", c=NCH)[:, :, t0:t0 + SEC]
                    ).then_inc(_dB, 16)
                P.on("sync", f_in)
                dB.n += 16
                wxin = dB.n

                for e in ("tensor", "scalar", "vector"):
                    def guard(eng, wxin=wxin):
                        eng.wait_ge(_dB, wxin)
                    P.on(e, guard)

                wbc1 = sec_stats(lambda c: xsec[:, c * SEC:(c + 1) * SEC],
                                 EPS_OUT, True)

                def v_ap(eng, wbc1=wbc1, t0=t0):
                    eng.wait_ge(_tT, wbc1)
                    for c in range(NCH):
                        eng.tensor_tensor(out=utmp, in0=xsec[:, c * SEC:(c + 1) * SEC],
                                          in1=ps4[:, 0:SEC], op=ALU.add)
                        ins = eng.tensor_tensor(
                            out=big_a[:, c * TPC + t0:c * TPC + t0 + SEC],
                            in0=utmp, in1=ps5[:, 0:SEC], op=ALU.mult)
                    ins.then_inc(_vV, 1)
                P.on("vector", v_ap)
                vV.n += 1
            return vV.n

        def stream_cast_w(src_ap3, j, home, home_war_t):
            for p in range(4):
                lo = p * 512
                slot = p % 2
                war = max(0, vW.n - 1)

                def f(eng, j=j, lo=lo, slot=slot, war=war):
                    eng.wait_ge(_vW, war)
                    eng.dma_start(out=wstage[:, slot * 512:slot * 512 + 512],
                                  in_=src_ap3[j, :, lo:lo + 512]).then_inc(dW[slot].h, 16)
                P.on("sync", f)
                dW[slot].n += 16
                w1 = dW[slot].n

                def g(eng, lo=lo, slot=slot, w1=w1, home=home, hw=home_war_t):
                    eng.wait_ge(dW[slot].h, w1)
                    eng.wait_ge(_tT, hw)
                    ins = eng.tensor_copy(home[:, lo:lo + 512],
                                          wstage[:, slot * 512:slot * 512 + 512])
                    ins.then_inc(_vW, 1)
                P.on("vector", g)
                vW.n += 1
            return vW.n

        def emit_attention(u2_vv):
            import os
            NP_EMIT = int(os.environ.get("P3_PAIRS", str(NPAIR)))
            STAGE = int(os.environ.get("P3_STAGE", "4"))
            BOPS = int(os.environ.get("P3_BOPS", "4"))
            wq_b = [wpool[:, 0:2048], wpool[:, 2048:4096]]
            wk_b = wpool[:, 4096:6144]
            wv_b = wpool[:, 6144:8192]
            qT = [parena[:, 0:2048], parena[:, 2048:4096]]
            kT = parena[:, 4096:6144]
            v_p = parena[:, 6144:8192]
            Pcur = [statef[:, 0:64], statef[:, 64:128]]
            sprev = [statef[:, 128:192], statef[:, 192:256]]
            sendt = statef[:, 256:320]
            ddm = dect[:, 0:256]
            kdm = dect[:, 256:384]
            qdm = [dect[:, 384:512], dect[:, 512:640]]
            kkd_blk = tmpb[:, 0:128]
            kkd_tm = tmpb[:, 128:256]
            aw_sb = tmpb[:, 256:512]
            Pb = tmpb[:, 512:640]
            Tst = tmpb[:, 640:704]
            qqd_a = tmpb[:, 704:832]
            qqd_b = tmpb[:, 832:960]
            Tst_f = scratch[:, 320:384]

            marks = {}
            pend_B = None

            def emit_B(j):
                m = marks[j]
                slot = j % 2
                dslot = j % 2
                wsp = m["sprev_dg"]
                wevac = m["attn_evac_ss"]
                for n in range(NBLK):
                    wq_ss = m["q_evac_ss"]
                    wtb_prev = tT.n

                    def vb1(eng, j=j, n=n, slot=slot, dslot=dslot, wsp=wsp,
                            wq_ss=wq_ss, wtb_prev=wtb_prev):
                        eng.wait_ge(_dG, wsp)
                        eng.wait_ge(_sS, wq_ss)
                        eng.wait_ge(_tT, wtb_prev)
                        eng.tensor_tensor(out=qqd_b, in0=qT[slot][:, n * 128:(n + 1) * 128],
                                          in1=qdm[dslot], op=ALU.mult)
                        if n == 0:
                            eng.tensor_scalar(
                                out=Tst_f, in0=sprev[slot],
                                scalar1=cst_f[:, RMOF:RMOF + 1],
                                scalar2=None, op0=ALU.mult)
                        else:
                            eng.tensor_scalar(
                                out=Tst_f, in0=Tst_f,
                                scalar1=cst_f[:, BDOF + j:BDOF + j + 1],
                                scalar2=None, op0=ALU.mult)
                        ins = eng.tensor_copy(Tst, Tst_f)
                        ins.then_inc(_vV, 1)
                    P.on("vector", vb1)
                    vV.n += 1
                    wv1 = vV.n
                    reg = n % 2
                    war_ps5 = marks[j].get("vb2", {}).get(n - 1, 0)

                    def tb1(eng, wv1=wv1, reg=reg, war_ps5=war_ps5):
                        eng.wait_ge(_vV, wv1)
                        if war_ps5:
                            eng.wait_ge(_vV, war_ps5)
                        eng.matmul(ps5[0:64, reg * 128:(reg + 1) * 128],
                                   Tst[0:64, :], qqd_b[0:64, :],
                                   start=True, stop=True, tile_position=(0, 0))
                        ins = eng.matmul(ps5[64:128, reg * 128:(reg + 1) * 128],
                                         Tst[64:128, :], qqd_b[64:128, :],
                                         start=True, stop=True, tile_position=(64, 64))
                        ins.then_inc(_tT, 1)
                    P.on("tensor", tb1)
                    tT.n += 1
                    wtb = tT.n

                    def vb2(eng, j=j, n=n, wtb=wtb, reg=reg, wevac=wevac):
                        eng.wait_ge(_tT, wtb)
                        eng.wait_ge(_sS, wevac)
                        ins = eng.tensor_tensor(
                            out=big_b[:, j * 2048 + n * 128:j * 2048 + (n + 1) * 128],
                            in0=big_b[:, j * 2048 + n * 128:j * 2048 + (n + 1) * 128],
                            in1=ps5[:, reg * 128:(reg + 1) * 128], op=ALU.add)
                        ins.then_inc(_vV, 1)
                    P.on("vector", vb2)
                    vV.n += 1
                    marks[j].setdefault("vb2", {})[n] = vV.n

            for j in range(NP_EMIT):
                slot = j % 2
                dslot = j % 2
                marks[j] = {}
                p3b = marks.get(0, {}).get("base_t", tT.n)
                if j == 0:
                    marks[0]["base_t"] = tT.n
                wq_war = marks.get(j - 2, {}).get("ph1_end_t", p3b if j < 2 else tT.n)
                a_war = marks.get(j - 1, {}).get("A_end_t", p3b if j < 1 else tT.n)
                wq_vw = stream_cast_w(ap["wq"], j, wq_b[slot], wq_war)
                wk_vw = stream_cast_w(ap["wk"], j, wk_b, a_war)
                wv_vw = stream_cast_w(ap["wv"], j, wv_b, a_war)
                war_vv = vV.n

                def f_dec(eng, j=j, dslot=dslot, war_vv=war_vv):
                    eng.wait_ge(_vV, war_vv)
                    eng.dma_start(out=ddm, in_=ap["ddt"][j, :, :]).then_inc(_dA, 16)
                    eng.dma_start(out=qdm[dslot], in_=ap["qdt"][j, :, :]).then_inc(_dA, 16)
                    eng.dma_start(out=kdm, in_=ap["kdt"][j, :, :]).then_inc(_dA, 16)
                P.on("sync", f_dec)
                dA.n += 48
                wdec = dA.n

                base_ss = sS.n
                tq_marks = []

                def t_ph1(eng, j=j, slot=slot, wq_vw=wq_vw, wk_vw=wk_vw,
                          wv_vw=wv_vw, u2_vv=u2_vv, base_ss=base_ss):
                    eng.wait_ge(_vV, u2_vv)
                    eng.wait_ge(_vW, wq_vw)
                    for tq in range(4):
                        psg = (ps0, ps1)[tq % 2]
                        eng.wait_ge(_sS, base_ss + tq - 1 if tq >= 2 else base_ss)
                        for ck in range(NCH):
                            ins = eng.matmul(psg[:, 0:512],
                                             wq_b[slot][:, ck * 128:(ck + 1) * 128],
                                             big_a[:, ck * TPC + tq * 512:ck * TPC + (tq + 1) * 512],
                                             start=(ck == 0), stop=(ck == NCH - 1))
                        ins.then_inc(_tT, 1)
                    eng.wait_ge(_vW, wk_vw)
                    for tq in range(4):
                        psg = (ps0, ps1)[tq % 2]
                        eng.wait_ge(_sS, base_ss + tq + 3)
                        for ck in range(NCH):
                            ins = eng.matmul(psg[:, 0:512],
                                             wk_b[:, ck * 128:(ck + 1) * 128],
                                             big_a[:, ck * TPC + tq * 512:ck * TPC + (tq + 1) * 512],
                                             start=(ck == 0), stop=(ck == NCH - 1))
                        ins.then_inc(_tT, 1)
                    eng.wait_ge(_vW, wv_vw)
                    for tb in range(NBLK):
                        psg = (ps0, ps1)[tb % 2]
                        eng.wait_ge(_sS, base_ss + tb + 7)
                        for ck in range(NCH):
                            ins = eng.matmul(psg[:, 0:128],
                                             big_a[:, ck * TPC + tb * 128:ck * TPC + (tb + 1) * 128],
                                             wv_b[:, ck * 128:(ck + 1) * 128],
                                             start=(ck == 0), stop=(ck == NCH - 1))
                        ins.then_inc(_tT, 1)
                P.on("tensor", t_ph1)
                for _i in range(24):
                    tT.n += 1
                    tq_marks.append(tT.n)
                marks[j]["ph1_end_t"] = tT.n

                def s_ph1(eng, j=j, slot=slot, tq_marks=list(tq_marks)):
                    for tq in range(4):
                        psg = (ps0, ps1)[tq % 2]
                        eng.wait_ge(_tT, tq_marks[tq])
                        ins = eng.activation(out=qT[slot][:, tq * 512:(tq + 1) * 512],
                                             in_=psg[:, 0:512], func=AF.Silu)
                        ins.then_inc(_sS, 1)
                    for tq in range(4):
                        psg = (ps0, ps1)[tq % 2]
                        eng.wait_ge(_tT, tq_marks[4 + tq])
                        ins = eng.activation(out=kT[:, tq * 512:(tq + 1) * 512],
                                             in_=psg[:, 0:512], func=AF.Silu)
                        ins.then_inc(_sS, 1)
                    for tb in range(NBLK):
                        psg = (ps0, ps1)[tb % 2]
                        eng.wait_ge(_tT, tq_marks[8 + tb])
                        ins = eng.activation(out=v_p[:, tb * 128:(tb + 1) * 128],
                                             in_=psg[:, 0:128], func=AF.Silu)
                        ins.then_inc(_sS, 1)
                P.on("scalar", s_ph1)
                for _i in range(24):
                    sS.n += 1
                marks[j]["q_evac_ss"] = sS.n
                ph1_evac_ss = sS.n

                if STAGE < 2:
                    continue
                def v_z(eng, slot=slot):
                    eng.memset(Pcur[slot], 0.0)
                    ins = eng.memset(Pb, 0.0)
                    ins.then_inc(_vV, 1)
                P.on("vector", v_z)
                vV.n += 1

                s1_marks = {}
                v6_marks = {}
                v4_marks = {}
                for n in range(NBLK):
                    reg = n % 2

                    def va(eng, j=j, n=n, slot=slot, dslot=dslot, wdec=wdec,
                           ph1=ph1_evac_ss, wtr=tT.n):
                        eng.wait_ge(_dA, wdec)
                        eng.wait_ge(_sS, ph1)
                        eng.wait_ge(_tT, wtr)
                        eng.tensor_copy(Pb[0:64, 0:64], Pcur[slot][0:64, :])
                        eng.tensor_copy(Pb[64:128, 64:128], Pcur[slot][64:128, :])
                        eng.tensor_tensor(out=qqd_a, in0=qT[slot][:, n * 128:(n + 1) * 128],
                                          in1=qdm[dslot], op=ALU.mult)
                        ins = eng.tensor_tensor(out=kkd_blk, in0=kT[:, n * 128:(n + 1) * 128],
                                                in1=kdm, op=ALU.mult)
                        ins.then_inc(_vV, 1)
                    P.on("vector", va)
                    vV.n += 1
                    wva = vV.n

                    def t_tr2(eng, wva=wva):
                        eng.wait_ge(_vV, wva)
                        ins = eng.transpose(out=pt0[:, 0:128], in_=kkd_blk,
                                            identity=cst_b[:, 0:128])
                        ins.then_inc(_tT, 1)
                    P.on("tensor", t_tr2)
                    tT.n += 1
                    wtr2 = tT.n

                    def v_ktm(eng, wtr2=wtr2):
                        eng.wait_ge(_tT, wtr2)
                        ins = eng.tensor_copy(kkd_tm, pt0[:, 0:128])
                        ins.then_inc(_vV, 1)
                    P.on("vector", v_ktm)
                    vV.n += 1
                    wktm = vV.n
                    war_ps3 = v6_marks.get(n - 1, 0)
                    war_ps2 = v4_marks.get(n - 1, 0)

                    def t_cw(eng, j=j, n=n, slot=slot, wktm=wktm,
                             war_ps3=war_ps3, war_ps2=war_ps2):
                        eng.wait_ge(_vV, wktm)
                        if war_ps3:
                            eng.wait_ge(_vV, war_ps3)
                        ins = None
                        if BOPS >= 2:
                            eng.matmul(ps3[0:64, 0:64], kkd_tm[:, 0:64],
                                       v_p[:, n * 128:n * 128 + 64],
                                       start=True, stop=True, tile_position=(0, 0))
                            ins = eng.matmul(ps3[64:128, 0:64], kkd_tm[:, 64:128],
                                             v_p[:, n * 128 + 64:(n + 1) * 128],
                                             start=True, stop=True, tile_position=(0, 64))
                        if BOPS >= 3:
                            if war_ps2:
                                eng.wait_ge(_vV, war_ps2)
                            eng.matmul(ps2[:, 0:128], kT[0:64, n * 128:(n + 1) * 128],
                                       qT[slot][0:64, n * 128:(n + 1) * 128],
                                       start=True, stop=True)
                            ins = eng.matmul(ps6[:, 0:128], kT[64:128, n * 128:(n + 1) * 128],
                                             qT[slot][64:128, n * 128:(n + 1) * 128],
                                             start=True, stop=True)
                        if ins is None:
                            ins = eng.matmul(ps3[0:1, 0:1], cst_b[0:1, 128:129], cst_b[0:1, 128:129],
                                             start=True, stop=True)
                        ins.then_inc(_tT, 1)
                    P.on("tensor", t_cw)
                    tT.n += 1
                    wcw = tT.n

                    def v_aw(eng, j=j, slot=slot, wcw=wcw):
                        eng.wait_ge(_tT, wcw)
                        if BOPS >= 3:
                            eng.tensor_tensor(out=aw_sb[:, 0:128], in0=ps2[:, 0:128],
                                              in1=ddm[:, 0:128], op=ALU.mult)
                            eng.tensor_tensor(out=aw_sb[:, 128:256], in0=ps6[:, 0:128],
                                              in1=ddm[:, 128:256], op=ALU.mult)
                        ins = eng.tensor_scalar(
                            out=Pcur[slot], in0=Pcur[slot],
                            scalar1=cst_f[:, BDOF + j:BDOF + j + 1],
                            scalar2=None, op0=ALU.mult)
                        ins.then_inc(_vV, 1)
                    P.on("vector", v_aw)
                    vV.n += 1
                    wvaw = vV.n
                    war_ps4 = s1_marks.get(n - 1, 0)

                    def t_ii(eng, j=j, n=n, slot=slot, reg=reg, wvaw=wvaw,
                             war_ps4=war_ps4):
                        eng.wait_ge(_vV, wvaw)
                        if war_ps4:
                            eng.wait_ge(_sS, war_ps4)
                        if BOPS < 4:
                            eng.matmul(ps4[0:1, 0:1], cst_b[0:1, 128:129], cst_b[0:1, 128:129],
                                       start=True, stop=True).then_inc(_tT, 1)
                            return
                        eng.matmul(ps4[0:64, reg * 128:(reg + 1) * 128],
                                   v_p[:, n * 128:n * 128 + 64], aw_sb[:, 0:128],
                                   start=True, stop=False, tile_position=(0, 0))
                        eng.matmul(ps4[0:64, reg * 128:(reg + 1) * 128],
                                   Pb[:, 0:64], qqd_a[:, :],
                                   start=False, stop=True, tile_position=(0, 0))
                        eng.matmul(ps4[64:128, reg * 128:(reg + 1) * 128],
                                   v_p[:, n * 128 + 64:(n + 1) * 128], aw_sb[:, 128:256],
                                   start=True, stop=False, tile_position=(0, 64))
                        ins = eng.matmul(ps4[64:128, reg * 128:(reg + 1) * 128],
                                         Pb[:, 64:128], qqd_a[:, :],
                                         start=False, stop=True, tile_position=(0, 64))
                        ins.then_inc(_tT, 1)
                    P.on("tensor", t_ii)
                    tT.n += 1
                    wii = tT.n

                    def v_pc(eng, slot=slot, wcw=wcw):
                        eng.wait_ge(_tT, wcw)
                        if BOPS >= 2:
                            ins = eng.tensor_tensor(out=Pcur[slot], in0=Pcur[slot],
                                                    in1=ps3[:, 0:64], op=ALU.add)
                        else:
                            ins = eng.engine_nop()
                        ins.then_inc(_vV, 1)
                    P.on("vector", v_pc)
                    vV.n += 1
                    v6_marks[n] = vV.n

                    def s_ev(eng, j=j, n=n, reg=reg, wii=wii):
                        eng.wait_ge(_tT, wii)
                        if BOPS >= 4:
                            ins = eng.activation(
                                out=big_b[:, j * 2048 + n * 128:j * 2048 + (n + 1) * 128],
                                in_=ps4[:, reg * 128:(reg + 1) * 128], func=AF.Copy)
                        else:
                            ins = eng.copy(big_b[:, j * 2048 + n * 128:j * 2048 + (n + 1) * 128],
                                           big_b[:, j * 2048 + n * 128:j * 2048 + (n + 1) * 128])
                        ins.then_inc(_sS, 1)
                    P.on("scalar", s_ev)
                    sS.n += 1
                    s1_marks[n] = sS.n
                    v4_marks[n] = wvaw
                marks[j]["attn_evac_ss"] = sS.n
                marks[j]["A_end_t"] = tT.n

                if STAGE < 3:
                    continue
                def v_send(eng, slot=slot):
                    ins = eng.tensor_scalar(out=sendt, in0=Pcur[slot],
                                            scalar1=cst_f[:, SMOF:SMOF + 1],
                                            scalar2=None, op0=ALU.mult)
                    ins.then_inc(_vV, 1)
                P.on("vector", v_send)
                vV.n += 1
                wsend = vV.n
                war_sprev = marks.get(j - 2, {}).get("B_done_vv", 0)

                def g_st(eng, j=j, slot=slot, wsend=wsend, g0=dG.n, c0=cC.n,
                         war_sprev=war_sprev):
                    eng.wait_ge(_vV, wsend)
                    eng.dma_start(out=st_in[j][:, :], in_=sendt).then_inc(_dG, 16)
                    eng.wait_ge(_dG, g0 + 16)
                    eng.collective_compute(
                        "AllReduce", ALU.add,
                        replica_groups=[[0, 1], [2, 3], [4, 5], [6, 7]],
                        ins=[st_in[j].opt()], outs=[st_out[j].opt()]).then_inc(_cC, 1)
                    eng.wait_ge(_cC, c0 + 1)
                    if war_sprev:
                        eng.wait_ge(_vV, war_sprev)
                    eng.dma_start(out=sprev[slot], in_=st_out[j][:, :]).then_inc(_dG, 16)
                P.on("gpsimd", g_st)
                dG.n += 32
                cC.n += 1
                marks[j]["sprev_dg"] = dG.n

                if int(os.environ.get("P3_BARRIER", "0")):
                    bt, bv, bs = tT.n, vV.n, sS.n

                    def barT(eng, bv=bv, bs=bs):
                        eng.wait_ge(_vV, bv)
                        eng.wait_ge(_sS, bs)
                        ins = eng.matmul(ps6[0:1, 0:1], cst_b[0:1, 128:129], cst_b[0:1, 128:129],
                                         start=True, stop=True)
                        ins.then_inc(_tT, 1)
                    P.on("tensor", barT)
                    tT.n += 1
                    bt2 = tT.n

                    def barV(eng, bt2=bt2, bs=bs):
                        eng.wait_ge(_tT, bt2)
                        eng.wait_ge(_sS, bs)
                        ins = eng.tensor_copy(tmpb[:, 0:32], tmpb[:, 0:32])
                        ins.then_inc(_vV, 1)
                    P.on("vector", barV)
                    vV.n += 1
                    bv2 = vV.n

                    def barS(eng, bt2=bt2, bv2=bv2):
                        eng.wait_ge(_tT, bt2)
                        eng.wait_ge(_vV, bv2)
                        ins = eng.copy(tmpb[:, 0:1], tmpb[:, 0:1])
                        ins.then_inc(_sS, 1)
                    P.on("scalar", barS)
                    sS.n += 1

                if STAGE >= 4:
                    if pend_B is not None:
                        emit_B(pend_B)
                        marks[pend_B]["B_done_vv"] = vV.n
                    pend_B = j
            if STAGE >= 4 and pend_B is not None:
                emit_B(pend_B)
                marks[pend_B]["B_done_vv"] = vV.n
            return vV.n

        def emit_p4(attn_vv):
            import os
            P4S = int(os.environ.get("P4_STAGE", "4"))
            P4O = int(os.environ.get("P4_OPS", "4"))
            rs_full = scratch[0:1, 0:2048]
            rsb_full = scratch[:, 2048:4096]
            yread = [scratch[:, 4096:4352], scratch[:, 4352:4608]]
            zwrite = [scratch[:, 4608:4864], scratch[:, 4864:5120]]
            wga_b = [wpool[:, 0:2048], wpool[:, 2048:4096]]
            wo_b = [wpool[:, 4096:6144], wpool[:, 6144:8192]]
            sqt4 = [parena[:, 0:256], parena[:, 256:512]]
            gat4 = [parena[:, 512:768], parena[:, 768:1024]]

            def v_guard(eng, attn_vv=attn_vv):
                eng.wait_ge(_vV, attn_vv)
            P.on("vector", v_guard)

            def s_guard(eng, attn_vv=attn_vv):
                eng.wait_ge(_vV, attn_vv)
            P.on("scalar", s_guard)

            def t_guard(eng, attn_vv=attn_vv):
                eng.wait_ge(_vV, attn_vv)
            P.on("tensor", t_guard)

            for s_i in range(NSEC if P4S >= 1 else 0):
                t0 = s_i * SEC
                base_t = tT.n
                base_s = sS.n

                def s_sq(eng, t0=t0, base_t=base_t):
                    for c in range(NCH):
                        eng.wait_ge(_tT, base_t + c - 1 if c >= 2 else base_t)
                        ins = eng.activation(out=sqt4[c % 2],
                                             in_=big_b[:, c * TPC + t0:c * TPC + t0 + SEC],
                                             func=AF.Square)
                        ins.then_inc(_sS, 1)
                P.on("scalar", s_sq)
                sS.n += NCH

                def t_ss(eng, base_s=base_s, base_v=vV.n):
                    eng.wait_ge(_vV, base_v)
                    for c in range(NCH):
                        eng.wait_ge(_sS, base_s + c + 1)
                        ins = eng.matmul(ps5[0:1, 0:SEC], cst_b[:, 128:129],
                                         sqt4[c % 2],
                                         start=(c == 0), stop=(c == NCH - 1))
                        ins.then_inc(_tT, 1)
                P.on("tensor", t_ss)
                tT.n += NCH
                wss = tT.n

                def v_pre(eng, wss=wss):
                    eng.wait_ge(_tT, wss)
                    ins = eng.tensor_scalar(out=miscf[0:1, UTMP:UTMP + SEC],
                                            in0=ps5[0:1, 0:SEC],
                                            scalar1=1.0 / DM, scalar2=float(EPS_RMS),
                                            op0=ALU.mult, op1=ALU.add)
                    ins.then_inc(_vV, 1)
                P.on("vector", v_pre)
                vV.n += 1
                wpre = vV.n

                def s_sq2(eng, wpre=wpre):
                    eng.wait_ge(_vV, wpre)
                    ins = eng.activation(out=miscf[0:1, RVAR:RVAR + SEC],
                                         in_=miscf[0:1, UTMP:UTMP + SEC], func=AF.Sqrt)
                    ins.then_inc(_sS, 1)
                P.on("scalar", s_sq2)
                sS.n += 1
                wsq2 = sS.n

                def v_rr(eng, t0=t0, wsq2=wsq2):
                    eng.wait_ge(_sS, wsq2)
                    ins = eng.reciprocal(out=rs_full[:, t0:t0 + SEC],
                                         in_=miscf[0:1, RVAR:RVAR + SEC])
                    ins.then_inc(_vV, 1)
                P.on("vector", v_rr)
                vV.n += 1
            wrs_full = vV.n

            if P4S < 2:
                return vV.n
            base_srb = sS.n

            def t_rb(eng, wrs_full=wrs_full, base_srb=base_srb):
                eng.wait_ge(_vV, wrs_full)
                for hq in range(NCH):
                    if hq >= 1:
                        eng.wait_ge(_sS, base_srb + hq)
                    ins = eng.matmul(ps4[:, (hq % 4) * 128:(hq % 4 + 1) * 128],
                                     onesr, rs_full[:, hq * 128:(hq + 1) * 128],
                                     start=True, stop=True)
                    ins.then_inc(_tT, 1)
            P.on("tensor", t_rb)
            trb = []
            for hq in range(NCH):
                tT.n += 1
                trb.append(tT.n)

            def s_rb(eng, trb=list(trb)):
                for hq in range(NCH):
                    eng.wait_ge(_tT, trb[hq])
                    ins = eng.copy(rsb_full[:, hq * 128:(hq + 1) * 128],
                                   ps4[:, (hq % 4) * 128:(hq % 4 + 1) * 128])
                    ins.then_inc(_sS, 1)
            P.on("scalar", s_rb)
            for hq in range(NCH):
                sS.n += 1

            gz_done = []
            tg_hist = {}
            p4_base_t = tT.n
            for cn in range(NCH if P4S >= 3 else 0):
                slot = cn % 2
                wga_vw = stream_cast_w(ap["wga"], cn, wga_b[slot],
                                       tg_hist.get(cn - 2, p4_base_t))
                base_s3 = sS.n
                base_v3 = vV.n
                tg_marks = []

                def t_g2(eng, cn=cn, slot=slot, wga_vw=wga_vw, base_s3=base_s3):
                    eng.wait_ge(_vW, wga_vw)
                    for s_i in range(NSEC):
                        psg = (ps0, ps1)[s_i % 2]
                        eng.wait_ge(_sS, base_s3 + s_i - 2 if s_i >= 2 else base_s3)
                        for ck in range(NCH):
                            ins = eng.matmul(psg[:, 0:SEC],
                                             wga_b[slot][:, ck * 128:(ck + 1) * 128],
                                             big_a[:, ck * TPC + s_i * SEC:ck * TPC + (s_i + 1) * SEC],
                                             start=(ck == 0), stop=(ck == NCH - 1))
                        ins.then_inc(_tT, 1)
                P.on("tensor", t_g2)
                for _i in range(NSEC):
                    tT.n += 1
                    tg_marks.append(tT.n)
                tg_hist[cn] = tT.n

                def s_g2(eng, tg_marks=list(tg_marks), base_v3=base_v3):
                    for s_i in range(NSEC):
                        psg = (ps0, ps1)[s_i % 2]
                        eng.wait_ge(_tT, tg_marks[s_i])
                        eng.wait_ge(_vV, base_v3 + s_i - 1 if s_i >= 2 else base_v3)
                        ins = eng.activation(out=gat4[s_i % 2], in_=psg[:, 0:SEC],
                                             func=AF.Sigmoid)
                        ins.then_inc(_sS, 1)
                P.on("scalar", s_g2)
                sg_marks = []
                for _i in range(NSEC):
                    sS.n += 1
                    sg_marks.append(sS.n)

                def v_gz(eng, cn=cn, sg_marks=list(sg_marks)):
                    for s_i in range(NSEC):
                        t0 = s_i * SEC
                        eng.wait_ge(_sS, sg_marks[s_i])
                        eng.tensor_tensor(out=sqt4[s_i % 2], in0=gat4[s_i % 2],
                                          in1=rsb_full[:, t0:t0 + SEC], op=ALU.mult)
                        ins = eng.tensor_tensor(
                            out=big_b[:, cn * TPC + t0:cn * TPC + t0 + SEC],
                            in0=big_b[:, cn * TPC + t0:cn * TPC + t0 + SEC],
                            in1=sqt4[s_i % 2], op=ALU.mult)
                        ins.then_inc(_vV, 1)
                P.on("vector", v_gz)
                for _i in range(NSEC):
                    vV.n += 1
                gz_done.append(vV.n)
            z_vv = vV.n

            to_hist = {}
            op_base_t = tT.n
            for cm in range(min(NCH, int(os.environ.get("P4_CM", str(NCH)))) if P4S >= 4 else 0):
                slot = cm % 2
                for p in range(4):
                    lo = p * 512
                    wslot = p % 2
                    war = max(0, vW.n - 1)

                    def f(eng, cm=cm, lo=lo, wslot=wslot, war=war):
                        eng.wait_ge(_vW, war)
                        eng.dma_start(out=wstage[:, wslot * 512:wslot * 512 + 512],
                                      in_=ap["wo"][cm, :, lo:lo + 512]).then_inc(dW[wslot].h, 16)
                    P.on("sync", f)
                    dW[wslot].n += 16
                    w1 = dW[wslot].n
                    hwar = to_hist.get(cm - 2, op_base_t)

                    def g(eng, cm=cm, lo=lo, wslot=wslot, w1=w1, slot=slot, hwar=hwar):
                        eng.wait_ge(dW[wslot].h, w1)
                        eng.wait_ge(_tT, hwar)
                        for q in range(4):
                            ck = (lo + q * 128) // 128
                            ins = eng.tensor_scalar(
                                out=wo_b[slot][:, lo + q * 128:lo + (q + 1) * 128],
                                in0=wstage[:, wslot * 512 + q * 128:wslot * 512 + (q + 1) * 128],
                                scalar1=cst_f[:, PPOF["normw_pp"] + ck:PPOF["normw_pp"] + ck + 1],
                                scalar2=None, op0=ALU.mult)
                        ins.then_inc(_vW, 1)
                    P.on("vector", g)
                    vW.n += 1
                wo_vw = vW.n

                base_v4 = vV.n
                to_marks = []

                def t_o(eng, cm=cm, slot=slot, wo_vw=wo_vw, z_vv=z_vv, base_v4=base_v4):
                    eng.wait_ge(_vW, wo_vw)
                    eng.wait_ge(_vV, z_vv)
                    for s_i in range(NSEC):
                        psg = (ps2, ps3)[s_i % 2]
                        eng.wait_ge(_vV, base_v4 + s_i - 1 if s_i >= 2 else base_v4)
                        for ck in range(NCH):
                            ins = eng.matmul(psg[:, 0:SEC],
                                             wo_b[slot][:, ck * 128:(ck + 1) * 128],
                                             big_b[:, ck * TPC + s_i * SEC:ck * TPC + (s_i + 1) * SEC],
                                             start=(ck == 0), stop=(ck == NCH - 1))
                        ins.then_inc(_tT, 1)
                P.on("tensor", t_o)
                for _i in range(NSEC):
                    tT.n += 1
                    to_marks.append(tT.n)
                to_hist[cm] = tT.n

                for s_i in range(NSEC):
                    rslot = s_i % 2
                    war_r = vV.n

                    if P4O >= 2:
                        def f_r(eng, cm=cm, s_i=s_i, rslot=rslot, war_r=war_r):
                            eng.wait_ge(_vV, war_r)
                            eng.dma_start(
                                out=yread[rslot],
                                in_=yres[:, cm * TPC + s_i * SEC:cm * TPC + (s_i + 1) * SEC]
                            ).then_inc(dR[rslot].h, 16)
                        P.on("sync", f_r)
                        dR[rslot].n += 16
                    wr = dR[rslot].n
                    wzw = dC.n

                    if P4O >= 3:
                        def v_o(eng, cm=cm, s_i=s_i, rslot=rslot, wr=wr,
                                to=to_marks[s_i], wzw=wzw):
                            eng.wait_ge(_tT, to)
                            eng.wait_ge(dR[rslot].h, wr)
                            eng.wait_ge(_dC, wzw)
                            psg = (ps2, ps3)[s_i % 2]
                            ins = eng.tensor_tensor(out=zwrite[rslot], in0=psg[:, 0:SEC],
                                                    in1=yread[rslot], op=ALU.add)
                            ins.then_inc(_vV, 1)
                        P.on("vector", v_o)
                        vV.n += 1
                    else:
                        def v_o(eng, to=to_marks[s_i], rslot=rslot):
                            eng.wait_ge(_tT, to)
                            ins = eng.tensor_copy(zwrite[rslot][:, 0:32],
                                                  zwrite[rslot][:, 0:32])
                            ins.then_inc(_vV, 1)
                        P.on("vector", v_o)
                        vV.n += 1
                    wvo = vV.n

                    if P4O >= 4:
                        def f_w(eng, cm=cm, s_i=s_i, rslot=rslot, wvo=wvo):
                            eng.wait_ge(_vV, wvo)
                            eng.dma_start(
                                out=zres[:, cm * TPC + s_i * SEC:cm * TPC + (s_i + 1) * SEC],
                                in_=zwrite[rslot]).then_inc(_dC, 16)
                            if s_i == NSEC - 1:
                                eng.dma_start(
                                    out=zl_in[0:1, cm * 128:(cm + 1) * 128],
                                    in_=zwrite[rslot][:, SEC - 1:SEC]).then_inc(_dC, 16)
                        P.on("sync", f_w)
                        dC.n += 16 * (2 if s_i == NSEC - 1 else 1)
            return vV.n

        load_thought_rows(1)
        stream_cast(ap["wagg"], 0, parena, 0, NCH * CSL)
        stream_cast(ap["wbc"], 0, parena, NCH * CSL, NCH * CSL)
        stream_cast(ap["wd1"], 0, big_b, 0, NCH * SLC)
        th_w1 = stream_cast(ap["wd2"], 0, big_b, NCH * SLC, (SLC // 128) * DM)
        bc1_vv = thought(1, BC1, th_w1)
        wgf_vw1 = stream_cast(ap["wgf"], 0, big_a, 0, NCH * DM)
        emit_ffn_sections(1, ap["x_img"],
                          out_img if stop_after == "P1" else yres,
                          BC1, wgf_vw1, bc1_vv)

        def dump_bf16(big, ready_vv):
            for p in range(8):
                lo = p * 4096
                wprev = dC.n

                def v_cp(eng, lo=lo, ready_vv=ready_vv, wprev=wprev):
                    eng.wait_ge(_vV, ready_vv)
                    eng.wait_ge(_dC, wprev)
                    ins = eng.tensor_copy(xsec, big[:, lo:lo + 4096])
                    ins.then_inc(_vV, 1)
                P.on("vector", v_cp)
                vV.n += 1
                wv = vV.n

                def f_cp(eng, lo=lo, wv=wv):
                    eng.wait_ge(_vV, wv)
                    eng.dma_start(out=out_img[:, lo:lo + 4096], in_=xsec).then_inc(_dC, 16)
                P.on("sync", f_cp)
                dC.n += 16

        if stop_after != "P1":
            u2_vv = emit_u2()
            if stop_after == "P2":
                dump_bf16(big_a, u2_vv)
            else:
                attn_vv = emit_attention(u2_vv)
                if stop_after == "P3":
                    dump_bf16(big_b, attn_vv)
        if stop_after not in ("P1", "P2", "P3"):
            p4_vv = emit_p4(attn_vv)
        if stop_after not in ("P1", "P2", "P3"):
            if stop_after == "P4":
                wzf = dC.n

                def f_cp(eng, wzf=wzf):
                    eng.wait_ge(_dC, wzf)
                    eng.dma_start(out=out_img[:, :], in_=zres[:, :]).then_inc(_dC, 16)
                P.on("sync", f_cp)
                dC.n += 16
            else:
                load_thought_rows(2)
                stream_cast(ap["wagg"], 0, parena, 0, NCH * CSL)
                stream_cast(ap["wbc"], 0, parena, NCH * CSL, NCH * CSL)
                stream_cast(ap["wd1"], 0, big_b, 0, NCH * SLC)
                th_w2 = stream_cast(ap["wd2"], 0, big_b, NCH * SLC, (SLC // 128) * DM)
                bc2_vv = thought(2, BC2, th_w2)
                wgf_vw2 = stream_cast(ap["wgf"], 0, big_a, 0, NCH * DM)
                emit_ffn_sections(2, zres, out_img, BC2, wgf_vw2, bc2_vv)

        wend = dC.n

        def f_end(eng, wend=wend):
            eng.wait_ge(_dC, wend)
        P.on("sync", f_end)

        P.run(blk)

    return nc


def kernel(**inputs):
    sh, per_core = host_prep(inputs)
    nc = build_nc()
    in_maps = []
    for c in range(8):
        m = dict(sh)
        m.update(per_core[c])
        in_maps.append(m)
    res = run_bass_kernel_spmd(nc, in_maps, core_ids=list(range(8)))
    out = np.zeros((4, 4096, 2048), np.float32)
    for c in range(8):
        b, h = c // 2, c % 2
        img = np.asarray(res.results[c]["out_img"])
        xt = img.reshape(128, NCH, TPC).transpose(1, 0, 2).reshape(DM, TPC)
        out[b, h * TPC:(h + 1) * TPC, :] = xt.T
    return out
